# version 02
# baseline (speedup 1.0000x reference)
import numpy as np
import contextlib

import numpy as np
import concourse.bass as bass
import concourse.bacc as bacc
import concourse.mybir as mybir

F32 = mybir.dt.float32
BF16 = mybir.dt.bfloat16
FP8 = mybir.dt.float8e4
I32 = mybir.dt.int32
AF = mybir.ActivationFunctionType
ALU = mybir.AluOpType
AX = mybir.AxisListType
DR = mybir.MatmulPerfMode.DoubleRow

NCORE = 8
S = 8192
DM = 2048
NH = 16
DH = 128
DI = 5632
TPC = S // NCORE
HPC = NH // NCORE

class Sem:
    def __init__(self, handle):
        self.h = handle
        self.n = 0
    def inc(self, k=1):
        self.n += k
        return k

def mk_sems(nc, names):
    import contextlib
    stack = contextlib.ExitStack()
    out = {}
    for n in names:
        out[n] = Sem(stack.enter_context(nc.semaphore(n)))
    return stack, out

class Counter:
    def __init__(self):
        self.n = 0
    def reserve(self, k=1):
        self.n += k
        return self.n


"""L1: token-sharded LN + qkv + unit-norm + rope + gain; FFN rowmax shards."""

NTC = TPC // 128
NF = DM // 128
NPIECE = (3 * DM) // 512
NG = NPIECE * NTC

def build_l1():
    nc = bacc.Bacc("TRN2")
    x = nc.dram_tensor("x", [TPC, DM], F32, kind="ExternalInput").ap()
    w_qkv = nc.dram_tensor("w_qkv", [DM, 3 * DM], F32, kind="ExternalInput").ap()
    cos_t = nc.dram_tensor("cos_t", [TPC, 64], F32, kind="ExternalInput").ap()
    sin_t = nc.dram_tensor("sin_t", [TPC, 64], F32, kind="ExternalInput").ap()
    gain_b = nc.dram_tensor("gain_b", [128, NH], F32, kind="ExternalInput").ap()
    ident = nc.dram_tensor("ident", [128, 128], BF16, kind="ExternalInput").ap()
    wg_sh = nc.dram_tensor("wg_sh", [DI // 8, DM], F32, kind="ExternalInput").ap()
    wu_sh = nc.dram_tensor("wu_sh", [DI // 8, DM], F32, kind="ExternalInput").ap()
    wd_sh = nc.dram_tensor("wd_sh", [DM // 8, DI], F32, kind="ExternalInput").ap()

    qkvn = nc.dram_tensor("qkvn", [TPC, 3 * DM], BF16, kind="ExternalOutput").ap()
    xn_o = nc.dram_tensor("xn_o", [TPC, DM], F32, kind="ExternalOutput").ap()
    rwg_o = nc.dram_tensor("rwg_o", [DI // 8], F32, kind="ExternalOutput").ap()
    rwu_o = nc.dram_tensor("rwu_o", [DI // 8], F32, kind="ExternalOutput").ap()
    rwd_o = nc.dram_tensor("rwd_o", [DM // 8], F32, kind="ExternalOutput").ap()

    c = {k: Counter() for k in
         ["mm", "ev", "sq", "red", "rsq", "app", "cstV", "cstS", "dqo0", "dqo1"]}
    G = []
    for piece in range(NPIECE):
        for tc in range(NTC):
            g = {"piece": piece, "tc": tc, "seg": piece // 4}
            g["mm"] = c["mm"].reserve()
            g["ev"] = c["ev"].reserve()
            if g["seg"] < 2:
                g["sq"] = c["sq"].reserve()
                g["red"] = c["red"].reserve()
                g["rsq"] = c["rsq"].reserve()
                g["app"] = c["app"].reserve()
                g["cst_sem"], g["cst"] = "cstS", c["cstS"].reserve()
            else:
                g["cst_sem"], g["cst"] = "cstV", c["cstV"].reserve()
            nn = len(G)
            g["dqo_sem"] = "dqo0" if nn % 2 == 0 else "dqo1"
            g["dqo"] = c[g["dqo_sem"]].reserve(16)
            G.append(g)
    cL = {k: Counter() for k in ["v", "s", "t", "evT"]}
    L = []
    for tc in range(NTC):
        d = {"xc": cL["v"].reserve(), "var": cL["v"].reserve(), "sqrt": cL["s"].reserve(),
             "xnr": cL["v"].reserve(), "xnb": cL["v"].reserve(),
             "tr": [cL["t"].reserve() for _ in range(NF)],
             "trev": [cL["evT"].reserve() for _ in range(NF)]}
        L.append(d)
    NCONST = 2 + 2 * NTC

    with contextlib.ExitStack() as st:
        e = st.enter_context
        xs = e(nc.sbuf_tensor("xs", [128, DM], F32))
        xn = [e(nc.sbuf_tensor(f"xn{i}", [128, DM], F32)) for i in range(2)]
        xc = e(nc.sbuf_tensor("xc", [128, DM], F32))
        sq_t = e(nc.sbuf_tensor("sq_t", [128, DM], F32))
        stat = e(nc.sbuf_tensor("stat", [128, 4], F32))
        rstd_all = e(nc.sbuf_tensor("rstd_all", [128, NTC], F32))
        xnb = e(nc.sbuf_tensor("xnb", [128, DM], BF16))
        xnT = e(nc.sbuf_tensor("xnT", [128, NTC * DM], BF16))
        idn = e(nc.sbuf_tensor("idn", [128, 128], BF16))
        cosb = e(nc.sbuf_tensor("cosb", [128, NTC * 64], F32))
        sinb = e(nc.sbuf_tensor("sinb", [128, NTC * 64], F32))
        gainb = e(nc.sbuf_tensor("gainb", [128, NH], F32))
        wb = e(nc.sbuf_tensor("wb", [128, NF * 1536], BF16))
        wk = [e(nc.sbuf_tensor(f"wk{i}", [128, 512], F32)) for i in range(2)]
        ssq = [e(nc.sbuf_tensor(f"ssq{i}", [128, 512], F32)) for i in range(2)]
        rn = [e(nc.sbuf_tensor(f"rn{i}", [128, 4], F32)) for i in range(2)]
        rtmp = e(nc.sbuf_tensor("rtmp", [128, 256], F32))
        stg = [e(nc.sbuf_tensor(f"stg{i}", [128, 512], BF16)) for i in range(2)]
        wsh = e(nc.sbuf_tensor("wsh", [128, DI], F32))
        rwcol = e(nc.sbuf_tensor("rwcol", [128, 16], F32))
        eps5 = e(nc.sbuf_tensor("eps5", [128, 1], F32))
        eps6 = e(nc.sbuf_tensor("eps6", [128, 1], F32))
        half_neg = e(nc.sbuf_tensor("half_neg", [128, 1], F32))
        ptr = [e(nc.psum_tensor(f"ptr{i}", [128, 128], BF16)) for i in range(2)]
        ps = [e(nc.psum_tensor(f"ps{i}", [128, 512], F32)) for i in range(6)]

        names = ["dx", "dw", "dout0", "dout1", "vL", "sL", "tT", "evT", "mm", "ev", "ssq_s",
                 "red", "rsq", "app", "cstV", "cstS", "dqo0", "dqo1", "rx"]
        stk, sm = mk_sems(nc, names)
        st.enter_context(stk)

        with nc.Block() as blk:
            @blk.vector
            def _(v):
                v.memset(eps5[:, :], 1e-5)
                v.memset(eps6[:, :], 1e-6)
                v.memset(half_neg[:, :], -0.5)

        with nc.Block() as blk:
            @blk.sync
            def _(sync):
                sync.dma_start(idn[:, :], ident[:, :]).then_inc(sm["dx"].h, sm["dx"].inc(16))
                sync.dma_start(gainb[:, :], gain_b[:, :]).then_inc(sm["dx"].h, sm["dx"].inc(16))
                for tc in range(NTC):
                    sync.dma_start(cosb[:, tc * 64:(tc + 1) * 64],
                                   cos_t[tc * 128:(tc + 1) * 128, :]).then_inc(sm["dx"].h, sm["dx"].inc(16))
                    sync.dma_start(sinb[:, tc * 64:(tc + 1) * 64],
                                   sin_t[tc * 128:(tc + 1) * 128, :]).then_inc(sm["dx"].h, sm["dx"].inc(16))
                for tc in range(NTC):
                    if tc >= 1:
                        sync.wait_ge(sm["vL"].h, L[tc - 1]["xc"])
                    sync.dma_start(xs[:, :], x[tc * 128:(tc + 1) * 128, :]).then_inc(sm["dx"].h, sm["dx"].inc(16))

            @blk.vector
            def _(v):
                for tc in range(NTC):
                    v.wait_ge(sm["dx"].h, 16 * (NCONST + tc + 1))
                    if tc >= 2:
                        v.wait_ge(sm["dout0" if tc % 2 == 0 else "dout1"].h, 16 * ((tc - 2) // 2 + 1))
                    v.tensor_reduce(stat[:, 0:1], xs[:, :], AX.X, ALU.add, negate=True)
                    v.drain()
                    v.tensor_scalar(stat[:, 0:1], stat[:, 0:1], 1.0 / DM, None, ALU.mult)
                    v.drain()
                    v.tensor_scalar(xc[:, :], xs[:, :], stat[:, 0:1], None, ALU.add)
                    v.drain()
                    v.sem_inc(sm["vL"].h, 1)
                    v.tensor_tensor(sq_t[:, :], xc[:, :], xc[:, :], ALU.mult)
                    v.drain()
                    v.tensor_reduce(stat[:, 1:2], sq_t[:, :], AX.X, ALU.add)
                    v.drain()
                    v.tensor_scalar(stat[:, 1:2], stat[:, 1:2], 1.0 / DM, None, ALU.mult)
                    v.drain()
                    v.sem_inc(sm["vL"].h, 1)
                    v.wait_ge(sm["sL"].h, L[tc]["sqrt"])
                    v.reciprocal(rstd_all[:, tc:tc + 1], stat[:, 2:3])
                    v.drain()
                    v.tensor_scalar(xn[tc % 2][:, :], xc[:, :], rstd_all[:, tc:tc + 1], None, ALU.mult)
                    v.drain()
                    v.sem_inc(sm["vL"].h, 1)
                    v.tensor_copy(xnb[:, :], xn[tc % 2][:, :])
                    v.drain()
                    v.sem_inc(sm["vL"].h, 1)
                    for f in range(NF):
                        v.wait_ge(sm["tT"].h, L[tc]["tr"][f])
                        v.tensor_copy(xnT[:, tc * DM + f * 128: tc * DM + (f + 1) * 128],
                                      ptr[f % 2][:, :]).then_inc(sm["evT"].h, 1)

            @blk.scalar
            def _(s):
                for tc in range(NTC):
                    s.wait_ge(sm["vL"].h, L[tc]["var"])
                    s.activation(stat[:, 2:3], stat[:, 1:2], AF.Sqrt, bias=eps5[:, 0:1], scale=1.0).then_inc(sm["sL"].h, 1)

            @blk.tensor
            def _(t):
                for tc in range(NTC):
                    t.wait_ge(sm["vL"].h, L[tc]["xnb"])
                    for f in range(NF):
                        n = tc * NF + f
                        if n >= 2:
                            tcp, fp = divmod(n - 2, NF)
                            t.wait_ge(sm["evT"].h, L[tcp]["trev"][fp])
                        t.transpose(ptr[f % 2][:, :], xnb[:, f * 128:(f + 1) * 128], idn[:, :]).then_inc(sm["tT"].h, 1)

            @blk.gpsimd
            def _(g):
                for f in range(NF):
                    g.dma_start(wb[:, f * 1536:(f + 1) * 1536],
                                w_qkv[f * 128:(f + 1) * 128, 0:1536]).then_inc(sm["dw"].h, sm["dw"].inc(16))
                for tc in range(NTC):
                    g.wait_ge(sm["vL"].h, L[tc]["xnr"])
                    dsem = sm["dout0"] if tc % 2 == 0 else sm["dout1"]
                    g.dma_start(xn_o[tc * 128:(tc + 1) * 128, :], xn[tc % 2][:, :]).then_inc(dsem.h, dsem.inc(16))

        with nc.Block() as blk:
            @blk.gpsimd
            def _(g):
                for q in range(1, 4):
                    g.wait_ge(sm["ev"].h, G[q * 24 - 1]["ev"])
                    for f in range(NF):
                        g.dma_start(wb[:, f * 1536:(f + 1) * 1536],
                                    w_qkv[f * 128:(f + 1) * 128, q * 1536:(q + 1) * 1536]).then_inc(sm["dw"].h, sm["dw"].inc(16))

            @blk.tensor
            def _(t):
                for n, g in enumerate(G):
                    piece, tc = g["piece"], g["tc"]
                    quarter, p_in = divmod(piece, 3)
                    t.wait_ge(sm["dw"].h, 16 * NF * (quarter + 1))
                    if n >= 6:
                        t.wait_ge(sm["ev"].h, G[n - 6]["ev"])
                    bank = n % 6
                    for f in range(NF):
                        m = t.matmul(ps[bank][:, :],
                                     xnT[:, tc * DM + f * 128:tc * DM + (f + 1) * 128],
                                     wb[:, f * 1536 + p_in * 512: f * 1536 + (p_in + 1) * 512],
                                     start=(f == 0), stop=(f == NF - 1))
                        if f == NF - 1:
                            m.then_inc(sm["mm"].h, 1)

            @blk.vector
            def _(v):
                for n, g in enumerate(G):
                    tc, seg = g["tc"], g["seg"]
                    b2 = n % 2
                    bank = n % 6
                    v.wait_ge(sm["mm"].h, g["mm"])
                    if n >= 2:
                        gp = G[n - 2]
                        v.wait_ge(sm[gp["cst_sem"]].h, gp["cst"])
                    if seg == 2:
                        if n >= 2:
                            v.wait_ge(sm[G[n - 2]["dqo_sem"]].h, G[n - 2]["dqo"])
                        v.tensor_copy(stg[b2][:, :], ps[bank][:, :])
                        v.drain()
                        v.sem_inc(sm["ev"].h, 1)
                        v.sem_inc(sm["cstV"].h, 1)
                    else:
                        v.tensor_copy(wk[b2][:, :], ps[bank][:, :])
                        v.drain()
                        v.sem_inc(sm["ev"].h, 1)
                        v.wait_ge(sm["ssq_s"].h, g["sq"])
                        for hs in range(4):
                            v.tensor_reduce(rn[b2][:, hs:hs + 1], ssq[b2][:, hs * 128:(hs + 1) * 128], AX.X, ALU.add)
                        v.drain()
                        v.sem_inc(sm["red"].h, 1)
                        v.wait_ge(sm["rsq"].h, g["rsq"])
                        h0 = (g["piece"] % 4) * 4
                        for hs in range(4):
                            h = h0 + hs
                            if seg == 0:
                                v.tensor_scalar(wk[b2][:, hs * 128:(hs + 1) * 128], wk[b2][:, hs * 128:(hs + 1) * 128],
                                                rn[b2][:, hs:hs + 1], gainb[:, h:h + 1], ALU.mult, ALU.mult)
                            else:
                                v.tensor_scalar(wk[b2][:, hs * 128:(hs + 1) * 128], wk[b2][:, hs * 128:(hs + 1) * 128],
                                                rn[b2][:, hs:hs + 1], None, ALU.mult)
                        v.drain()
                        co = cosb[:, tc * 64:(tc + 1) * 64]
                        si = sinb[:, tc * 64:(tc + 1) * 64]
                        for hs in (0, 2):
                            sl = wk[b2][:, hs * 128:(hs + 1) * 128]
                            t1_, t2_ = sl[:, 0:64], sl[:, 64:128]
                            A, B = rtmp[:, 0:64], rtmp[:, 64:128]
                            C, D = rtmp[:, 128:192], rtmp[:, 192:256]
                            v.tensor_tensor(A, t1_, co, ALU.mult)
                            v.tensor_tensor(B, t2_, si, ALU.mult)
                            v.tensor_tensor(C, t1_, si, ALU.mult)
                            v.tensor_tensor(D, t2_, co, ALU.mult)
                            v.drain()
                            v.tensor_tensor(t1_, A, B, ALU.subtract)
                            v.tensor_tensor(t2_, C, D, ALU.add)
                            v.drain()
                        v.sem_inc(sm["app"].h, 1)

            @blk.scalar
            def _(s):
                for n, g in enumerate(G):
                    if g["seg"] == 2:
                        continue
                    b2 = n % 2
                    s.wait_ge(sm["ev"].h, g["ev"])
                    s.square(ssq[b2][:, :], wk[b2][:, :]).then_inc(sm["ssq_s"].h, 1)
                    s.wait_ge(sm["red"].h, g["red"])
                    s.activation(rn[b2][:, 0:4], rn[b2][:, 0:4], AF.Ln, bias=eps6[:, 0:1], scale=1.0)
                    s.drain()
                    s.activation(rn[b2][:, 0:4], rn[b2][:, 0:4], AF.Exp, bias=0.0, scale=half_neg[:, 0:1]).then_inc(sm["rsq"].h, 1)
                    s.wait_ge(sm["app"].h, g["app"])
                    if n >= 2:
                        s.wait_ge(sm[G[n - 2]["dqo_sem"]].h, G[n - 2]["dqo"])
                    s.copy(stg[b2][:, :], wk[b2][:, :]).then_inc(sm["cstS"].h, 1)

            @blk.sync
            def _(sync):
                for n, g in enumerate(G):
                    piece, tc = g["piece"], g["tc"]
                    sync.wait_ge(sm[g["cst_sem"]].h, g["cst"])
                    dq = sm[g["dqo_sem"]]
                    sync.dma_start(qkvn[tc * 128:(tc + 1) * 128, piece * 512:(piece + 1) * 512],
                                   stg[n % 2][:, :]).then_inc(dq.h, dq.inc(16))

        shard_specs = [(wg_sh, rwg_o, DI // 8, DM), (wu_sh, rwu_o, DI // 8, DM), (wd_sh, rwd_o, DM // 8, DI)]
        tiles = []
        for si, (src, dst, nrows, width) in enumerate(shard_specs):
            for i in range((nrows + 127) // 128):
                tiles.append((si, i, min(128, nrows - i * 128), width))

        with nc.Block() as blk:
            @blk.sync
            def _(sync):
                for k, (si, i, rr, width) in enumerate(tiles):
                    src = shard_specs[si][0]
                    if k >= 1:
                        sync.wait_ge(sm["rx"].h, k)
                    sync.dma_start(wsh[:rr, :width], src[i * 128:i * 128 + rr, :]).then_inc(sm["dx"].h, sm["dx"].inc(16))

            @blk.vector
            def _(v):
                base = sm["dx"].n - 16 * len(tiles)
                for k, (si, i, rr, width) in enumerate(tiles):
                    v.wait_ge(sm["dx"].h, base + 16 * (k + 1))
                    v.tensor_reduce(rwcol[:rr, k:k + 1], wsh[:rr, :width], AX.X, ALU.max,
                                    apply_absolute_value=True)
                    v.drain()
                    v.sem_inc(sm["rx"].h, 1)
                    sm["rx"].inc()

            @blk.gpsimd
            def _(g):
                g.wait_ge(sm["rx"].h, len(tiles))
                for k, (si, i, rr, width) in enumerate(tiles):
                    dst = shard_specs[si][1]
                    g.dma_start(dst[i * 128:i * 128 + rr], rwcol[:rr, k:k + 1]).then_inc(sm["dout0"].h, sm["dout0"].inc(16))
                g.wait_ge(sm["dout0"].h, sm["dout0"].n)
                g.wait_ge(sm["dout1"].h, sm["dout1"].n)
                g.wait_ge(sm["dqo0"].h, sm["dqo0"].n)
                g.wait_ge(sm["dqo1"].h, sm["dqo1"].n)
    return nc

"""L2: head-sharded causal attention. Per core: 2 heads, all 8192 tokens.

q_fm, k_fm: [2, 128, 8192] bf16 (feature-major, unit-normed, roped, q has gain)
v_tm: [8192, 256] bf16
mask01, mask23: [128, 1024] bf16 causal masks for diagonal 2-kb groups
out y: [2, 128, 8192] bf16 (normalized attention out, feature-major)
"""

NQB = S // 512

def build_l2():
    nc = bacc.Bacc("TRN2")
    q_in = nc.dram_tensor("q_in", [HPC, 128, S], BF16, kind="ExternalInput").ap()
    k_in = nc.dram_tensor("k_in", [HPC, 128, S], BF16, kind="ExternalInput").ap()
    v_in = nc.dram_tensor("v_in", [S, HPC * 128], BF16, kind="ExternalInput").ap()
    m01 = nc.dram_tensor("m01", [128, 1024], BF16, kind="ExternalInput").ap()
    m23 = nc.dram_tensor("m23", [128, 1024], BF16, kind="ExternalInput").ap()
    y_o = nc.dram_tensor("y_o", [HPC, 128, S], BF16, kind="ExternalOutput").ap()

    c = {k: Counter() for k in ["mm", "exp", "acc", "av", "dn", "rcp", "bc", "yn", "dy0", "dy1"]}
    HQ = []
    for h in range(HPC):
        for qb in range(NQB):
            grps = []
            for gi in range(2 * (qb + 1)):
                grps.append({"mm": c["mm"].reserve(), "exp": c["exp"].reserve(),
                             "acc": c["acc"].reserve(), "av": c["av"].reserve()})
            d = {"h": h, "qb": qb, "grps": grps,
                 "dn": c["dn"].reserve(), "rcp": c["rcp"].reserve(), "bc": c["bc"].reserve(),
                 "yn": c["yn"].reserve()}
            nhq = len(HQ)
            d["dy_sem"] = "dy0" if nhq % 2 == 0 else "dy1"
            d["dy"] = c[d["dy_sem"]].reserve(16)
            HQ.append(d)

    with contextlib.ExitStack() as st:
        e = st.enter_context
        qf = e(nc.sbuf_tensor("qf", [128, HPC * S], BF16))
        kf = e(nc.sbuf_tensor("kf", [128, HPC * S], BF16))
        vt = e(nc.sbuf_tensor("vt", [128, (S // 128) * HPC * 128], BF16))
        msk = e(nc.sbuf_tensor("msk", [128, 2048], BF16))
        attn = [e(nc.sbuf_tensor(f"attn{i}", [128, 1024], BF16)) for i in range(2)]
        acc = e(nc.sbuf_tensor("acc", [128, 1024], BF16))
        dnf = e(nc.sbuf_tensor("dnf", [128, 4], F32))
        dn512 = e(nc.sbuf_tensor("dn512", [1, 512], F32))
        rcb = e(nc.sbuf_tensor("rcb", [1, 512], BF16))
        ones_col = e(nc.sbuf_tensor("ones_col", [128, 1], BF16))
        ones_row = e(nc.sbuf_tensor("ones_row", [1, 128], BF16))
        rnorm = e(nc.sbuf_tensor("rnorm", [128, 512], F32))
        ystg = [e(nc.sbuf_tensor(f"ystg{i}", [128, 512], BF16)) for i in range(2)]
        psqk = [e(nc.psum_tensor(f"psqk{i}", [128, 1024], F32)) for i in range(2)]
        ps_y = e(nc.psum_tensor("ps_y", [128, 512], F32))
        ps_d = e(nc.psum_tensor("ps_d", [1, 512], F32))
        ps_b = e(nc.psum_tensor("ps_b", [128, 512], F32))

        names = ["din", "mm", "expd", "accd", "avd", "dnd", "rcpd", "bcd", "rnd", "yst", "dy0", "dy1"]
        stk, sm = mk_sems(nc, names)
        st.enter_context(stk)

        with nc.Block() as blk:
            @blk.vector
            def _(v):
                v.memset(ones_col[:, :], 1.0)
                v.memset(ones_row[:, :], 1.0)

        with nc.Block() as blk:
            @blk.sync
            def _(sync):
                for h in range(HPC):
                    sync.dma_start(qf[:, h * S:(h + 1) * S], q_in[h, :, :]).then_inc(sm["din"].h, sm["din"].inc(16))
                    sync.dma_start(kf[:, h * S:(h + 1) * S], k_in[h, :, :]).then_inc(sm["din"].h, sm["din"].inc(16))
                for tb in range(S // 128):
                    sync.dma_start(vt[:, tb * 256:(tb + 1) * 256],
                                   v_in[tb * 128:(tb + 1) * 128, :]).then_inc(sm["din"].h, sm["din"].inc(16))
                sync.dma_start(msk[:, 0:1024], m01[:, :]).then_inc(sm["din"].h, sm["din"].inc(16))
                sync.dma_start(msk[:, 1024:2048], m23[:, :]).then_inc(sm["din"].h, sm["din"].inc(16))

        DIN_ALL = 16 * (2 * HPC + S // 128 + 2)

        with nc.Block() as blk:
            @blk.tensor
            def _(t):
                t.wait_ge(sm["din"].h, DIN_ALL)
                for nhq, d in enumerate(HQ):
                    h, qb = d["h"], d["qb"]
                    ng = len(d["grps"])
                    for gi, grp in enumerate(d["grps"]):
                        pp = psqk[gi % 2]
                        if grp["mm"] > 2:
                            t.wait_ge(sm["expd"].h, grp["mm"] - 2)
                        if gi == 0 and nhq >= 1:
                            t.wait_ge(sm["yst"].h, nhq)
                        for j in range(2):
                            kb = gi * 2 + j
                            m = t.matmul(pp[:, j * 512:(j + 1) * 512],
                                         kf[:, h * S + kb * 128: h * S + (kb + 1) * 128],
                                         qf[:, h * S + qb * 512: h * S + (qb + 1) * 512],
                                         start=True, stop=True)
                            if j == 1:
                                m.then_inc(sm["mm"].h, 1)
                        t.wait_ge(sm["accd"].h, grp["acc"])
                        for j in range(2):
                            kb = gi * 2 + j
                            m = t.matmul(ps_y[:, :],
                                         vt[:, kb * 256 + h * 128: kb * 256 + (h + 1) * 128],
                                         attn[gi % 2][:, j * 512:(j + 1) * 512],
                                         start=(gi == 0 and j == 0), stop=(gi == ng - 1 and j == 1))
                            if j == 1:
                                m.then_inc(sm["avd"].h, 1)
                    t.wait_ge(sm["accd"].h, d["grps"][-1]["acc"])
                    if nhq >= 1:
                        t.wait_ge(sm["rcpd"].h, nhq)
                    t.matmul(ps_d[:, :], ones_col[:, :], acc[:, 0:512], start=True, stop=False)
                    t.matmul(ps_d[:, :], ones_col[:, :], acc[:, 512:1024], start=False, stop=True).then_inc(sm["dnd"].h, 1)
                    t.wait_ge(sm["rcpd"].h, d["rcp"])
                    t.matmul(ps_b[:, :], ones_row[:, :], rcb[:, :], start=True, stop=True).then_inc(sm["bcd"].h, 1)

            @blk.scalar
            def _(s):
                for d in HQ:
                    for gi, grp in enumerate(d["grps"]):
                        s.wait_ge(sm["mm"].h, grp["mm"])
                        if grp["mm"] > 2:
                            s.wait_ge(sm["avd"].h, grp["mm"] - 2)
                        s.activation(attn[gi % 2][:, :], psqk[gi % 2][:, :], AF.Exp,
                                     bias=0.0, scale=1.0).then_inc(sm["expd"].h, 1)
                    s.wait_ge(sm["bcd"].h, d["bc"])
                    s.copy(rnorm[:, :], ps_b[:, :]).then_inc(sm["rnd"].h, 1)

            @blk.vector
            def _(v):
                for nhq, d in enumerate(HQ):
                    ng = len(d["grps"])
                    v.memset(acc[:, :], 0)
                    v.drain()
                    for gi, grp in enumerate(d["grps"]):
                        v.wait_ge(sm["expd"].h, grp["exp"])
                        if gi >= ng - 2:
                            mslice = msk[:, 0:1024] if gi == ng - 2 else msk[:, 1024:2048]
                            v.tensor_tensor(attn[gi % 2][:, :], attn[gi % 2][:, :], mslice, ALU.mult)
                            v.drain()
                        v.tensor_tensor(acc[:, :], acc[:, :], attn[gi % 2][:, :], ALU.add)
                        v.drain()
                        v.sem_inc(sm["accd"].h, 1)
                    v.wait_ge(sm["dnd"].h, d["dn"])
                    v.reciprocal(dn512[:, :], ps_d[:, :])
                    v.drain()
                    v.tensor_copy(rcb[:, :], dn512[:, :])
                    v.drain()
                    v.sem_inc(sm["rcpd"].h, 1)
                    v.wait_ge(sm["avd"].h, d["grps"][-1]["av"])
                    v.wait_ge(sm["rnd"].h, d["yn"])
                    if nhq >= 2:
                        gp = HQ[nhq - 2]
                        v.wait_ge(sm[gp["dy_sem"]].h, gp["dy"])
                    v.tensor_tensor(ystg[nhq % 2][:, :], ps_y[:, :], rnorm[:, :], ALU.mult)
                    v.drain()
                    v.sem_inc(sm["yst"].h, 1)

            @blk.gpsimd
            def _(g):
                for nhq, d in enumerate(HQ):
                    h, qb = d["h"], d["qb"]
                    g.wait_ge(sm["yst"].h, nhq + 1)
                    dq = sm[d["dy_sem"]]
                    g.dma_start(y_o[h, :, qb * 512:(qb + 1) * 512],
                                ystg[nhq % 2][:, :]).then_inc(dq.h, dq.inc(16))
                g.wait_ge(sm["dy0"].h, sm["dy0"].n)
                g.wait_ge(sm["dy1"].h, sm["dy1"].n)
    return nc

"""L3: token-sharded FFN (fp8) + attention out-projection + final residual."""

NTC = TPC // 128
NIP = DI // 512
NFP = DM // 256
NP2 = DI // 256
NHP = DM // 512
NDC = DM // 128
Q2 = 4.0 / (448.0 * 448.0)

def build_l3():
    nc = bacc.Bacc("TRN2")
    x = nc.dram_tensor("x", [TPC, DM], F32, kind="ExternalInput").ap()
    xn = nc.dram_tensor("xn", [TPC, DM], F32, kind="ExternalInput").ap()
    yT = nc.dram_tensor("yT", [DM, TPC], BF16, kind="ExternalInput").ap()
    w_out = nc.dram_tensor("w_out", [DM, DM], F32, kind="ExternalInput").ap()
    wgT = nc.dram_tensor("wgT", [DM, DI], F32, kind="ExternalInput").ap()
    wuT = nc.dram_tensor("wuT", [DM, DI], F32, kind="ExternalInput").ap()
    wd2T = nc.dram_tensor("wd2T", [DI, DM], F32, kind="ExternalInput").ap()
    rwg_i = nc.dram_tensor("rwg_i", [1, DI], F32, kind="ExternalInput").ap()
    rwu_i = nc.dram_tensor("rwu_i", [1, DI], F32, kind="ExternalInput").ap()
    rwd_i = nc.dram_tensor("rwd_i", [1, DM], F32, kind="ExternalInput").ap()
    id8 = nc.dram_tensor("id8", [128, 128], FP8, kind="ExternalInput").ap()
    out_o = nc.dram_tensor("out_o", [TPC, DM], F32, kind="ExternalOutput").ap()
    t_spill = nc.dram_tensor("t_spill", [TPC, DI], F32, kind="Internal").ap()

    st = contextlib.ExitStack()
    e = st.enter_context
    with st:
        xw = e(nc.sbuf_tensor("xw", [128, DM], F32))
        rm_sc = e(nc.sbuf_tensor("rm_sc", [128, 2 * NTC], F32))
        rmt_raw = e(nc.sbuf_tensor("rmt_raw", [128, NTC], F32))
        rmt_sc = e(nc.sbuf_tensor("rmt_sc", [128, 2 * NTC], F32))
        rmc = e(nc.sbuf_tensor("rmc", [128, 2], F32))
        id8b = e(nc.sbuf_tensor("id8b", [128, 128], FP8))
        ws = [e(nc.sbuf_tensor(f"ws{i}", [128, 1024], F32)) for i in range(2)]
        qwf = e(nc.sbuf_tensor("qwf", [128, NP2 * 1024], FP8))
        rowp = e(nc.sbuf_tensor("rowp", [1, 1024], F32))
        binv = e(nc.sbuf_tensor("binv", [128, 512], F32))
        boutg = e(nc.sbuf_tensor("boutg", [128, 512], F32))
        boutu = e(nc.sbuf_tensor("boutu", [128, 512], F32))
        tseg = [e(nc.sbuf_tensor(f"tseg{i}", [128, 512], F32)) for i in range(2)]
        y2 = e(nc.sbuf_tensor("y2", [128, NTC * DM], F32))
        ostg = [e(nc.sbuf_tensor(f"ostg{i}", [128, DM], F32)) for i in range(2)]
        ps = [e(nc.psum_tensor(f"ps{i}", [128, 512], F32)) for i in range(7)]

        names = ["dx", "did", "drp", "vA", "tA", "evA",
                 "dw0", "dw1", "qwB", "mmB", "evB", "silB", "tdB", "dsp0", "dsp1", "bcG",
                 "dtC", "vC", "tC", "evC",
                 "qwD", "mmD", "evD", "bcD",
                 "dwE0", "dwE1", "dyE0", "dyE1", "bufE", "mmE", "evE",
                 "dxF", "vF", "doF0", "doF1"]
        stk, sm = mk_sems(nc, names)
        e(stk)

        def dr_lhsT(tens, width, base):
            return bass.AP(tens, base, [[width, 128], [128, 2], [1, 128]])

        def dr_rhs(slot):
            return bass.AP(qwf, slot * 1024, [[NP2 * 1024, 128], [512, 2], [1, 512]])

        stA = contextlib.ExitStack()
        qa = stA.enter_context(nc.sbuf_tensor("qa", [128, DM], FP8))
        stPA = contextlib.ExitStack()
        pst = stPA.enter_context(nc.psum_tensor("pstA", [128, 256], FP8))
        qaT_t = stA.enter_context(nc.sbuf_tensor("qaT", [128, NTC * DM], FP8))
        gbuf = stA.enter_context(nc.sbuf_tensor("gbuf", [128, NTC * 512], F32))
        sil = stA.enter_context(nc.sbuf_tensor("sil", [128, 512], F32))
        ubuf = stA.enter_context(nc.sbuf_tensor("ubuf", [128, 512], F32))

        with nc.Block() as blk:
            @blk.sync
            def _(sync):
                sync.dma_start(id8b[:, :], id8[:, :]).then_inc(sm["did"].h, sm["did"].inc(16))
                for tc in range(NTC):
                    if tc >= 1:
                        sync.wait_ge(sm["vA"].h, 3 * tc)
                    sync.dma_start(xw[:, :], xn[tc * 128:(tc + 1) * 128, :]).then_inc(sm["dx"].h, sm["dx"].inc(16))

            @blk.vector
            def _(v):
                for tc in range(NTC):
                    v.wait_ge(sm["dx"].h, 16 * (1 + tc))
                    v.tensor_reduce(rmc[:, 0:1], xw[:, :], AX.X, ALU.max, apply_absolute_value=True)
                    v.drain()
                    v.reciprocal(rmc[:, 1:2], rmc[:, 0:1])
                    v.drain()
                    v.tensor_scalar(rm_sc[:, 2 * tc:2 * tc + 1], rmc[:, 1:2], 224.0, None, ALU.mult)
                    v.tensor_scalar(rm_sc[:, 2 * tc + 1:2 * tc + 2], rmc[:, 0:1], Q2, None, ALU.mult)
                    v.drain()
                    v.tensor_scalar(qa[:, :], xw[:, :], rm_sc[:, 2 * tc:2 * tc + 1], None, ALU.mult)
                    v.drain()
                    v.sem_inc(sm["vA"].h, 3)
                    for f in range(NDC):
                        v.wait_ge(sm["tA"].h, tc * NDC + f + 1)
                        fp, i = divmod(f, 2)
                        v.tensor_copy(qaT_t[:, tc * DM + fp * 256 + i * 128: tc * DM + fp * 256 + (i + 1) * 128],
                                      bass.AP(pst, 0, [[256, 128], [2, 128]])).then_inc(sm["evA"].h, 1)

            @blk.tensor
            def _(t):
                t.wait_ge(sm["did"].h, 16)
                for tc in range(NTC):
                    t.wait_ge(sm["vA"].h, 3 * tc + 3)
                    for f in range(NDC):
                        n = tc * NDC + f
                        if n >= 1:
                            t.wait_ge(sm["evA"].h, n)
                        t.transpose(bass.AP(pst, 0, [[256, 128], [2, 128]]),
                                    qa[:, f * 128:(f + 1) * 128], id8b[:, :]).then_inc(sm["tA"].h, 1)

        stPA.close()
        stPB = contextlib.ExitStack()
        ps8 = stPB.enter_context(nc.psum_tensor("ps8b", [128, 512], F32))
        PS = ps + [ps8]

        cdw = {0: Counter(), 1: Counter()}
        cqw = Counter(); cmm = Counter(); cev = Counter()
        csl = Counter(); ctd = Counter(); crp = Counter()
        SCH = []
        for ip in range(NIP):
            d = {"ip": ip, "rp": crp.reserve(32)}
            for w in ("g", "u"):
                dws, qws = [], []
                for fp in range(NFP):
                    par = fp % 2
                    dws.append((f"dw{par}", cdw[par].reserve(32)))
                    qws.append(cqw.reserve())
                d["dw" + w] = dws
                d["qw" + w] = qws
                d["mm" + w] = [cmm.reserve() for _ in range(NTC)]
            d["ev"] = [cev.reserve() for _ in range(NTC)]
            d["sil"] = [csl.reserve() for _ in range(NTC)]
            d["td"] = [ctd.reserve() for _ in range(NTC)]
            SCH.append(d)

        with nc.Block() as blk:
            @blk.sync
            def _(sync):
                for d in SCH:
                    ip = d["ip"]
                    if ip >= 1:
                        sync.wait_ge(sm["bcG"].h, ip)
                    sync.dma_start(rowp[:, 0:512], rwg_i[:, ip * 512:(ip + 1) * 512]).then_inc(sm["drp"].h, 16)
                    sync.dma_start(rowp[:, 512:1024], rwu_i[:, ip * 512:(ip + 1) * 512]).then_inc(sm["drp"].h, 16)

            @blk.gpsimd
            def _(g):
                for d in SCH:
                    ip = d["ip"]
                    g.wait_ge(sm["drp"].h, d["rp"])
                    if ip > 0:
                        g.wait_ge(sm["qwB"].h, SCH[ip - 1]["qwu"][-1])
                        g.wait_ge(sm["tdB"].h, SCH[ip - 1]["td"][-1])
                    g.partition_broadcast(boutg[:, :], rowp[0:1, 0:512], channels=128)
                    g.partition_broadcast(boutu[:, :], rowp[0:1, 512:1024], channels=128)
                    g.drain()
                    g.sem_inc(sm["bcG"].h, 1)
                    for w, src in (("g", wgT), ("u", wuT)):
                        for fp in range(NFP):
                            buf = fp % 2
                            qm = d["qw" + w][fp]
                            if qm - 2 >= 1:
                                g.wait_ge(sm["qwB"].h, qm - 2)
                            dsn, dsv = d["dw" + w][fp]
                            g.dma_start(ws[buf][:, 0:512],
                                        src[fp * 256:fp * 256 + 128, ip * 512:(ip + 1) * 512]).then_inc(sm[dsn].h, 16)
                            g.dma_start(ws[buf][:, 512:1024],
                                        src[fp * 256 + 128:fp * 256 + 256, ip * 512:(ip + 1) * 512]).then_inc(sm[dsn].h, 16)

            @blk.vector
            def _(v):
                for d in SCH:
                    ip = d["ip"]
                    v.wait_ge(sm["bcG"].h, ip + 1)
                    for w in ("g", "u"):
                        if w == "u":
                            v.wait_ge(sm["mmB"].h, d["mmg"][-1])
                        elif ip > 0:
                            v.wait_ge(sm["mmB"].h, SCH[ip - 1]["mmu"][-1])
                        bout = boutg if w == "g" else boutu
                        v.reciprocal(binv[:, :], bout[:, :])
                        v.drain()
                        v.tensor_scalar(binv[:, :], binv[:, :], 224.0, None, ALU.mult)
                        v.drain()
                        for fp in range(NFP):
                            buf = fp % 2
                            dsn, dsv = d["dw" + w][fp]
                            v.wait_ge(sm[dsn].h, dsv)
                            v.tensor_tensor(qwf[:, fp * 1024:fp * 1024 + 512], ws[buf][:, 0:512], binv[:, :], ALU.mult)
                            v.tensor_tensor(qwf[:, fp * 1024 + 512:(fp + 1) * 1024], ws[buf][:, 512:1024], binv[:, :], ALU.mult)
                            v.drain()
                            v.sem_inc(sm["qwB"].h, 1)
                    for tc in range(NTC):
                        v.wait_ge(sm["mmB"].h, d["mmg"][tc])
                        v.scalar_tensor_tensor(gbuf[:, tc * 512:(tc + 1) * 512], PS[tc][:, :],
                                               rm_sc[:, 2 * tc + 1:2 * tc + 2], boutg[:, :],
                                               ALU.mult, ALU.mult)
                        v.drain()
                        v.sem_inc(sm["evB"].h, 1)
                    for tc in range(NTC):
                        v.wait_ge(sm["mmB"].h, d["mmu"][tc])
                        v.scalar_tensor_tensor(ubuf[:, :], PS[tc][:, :],
                                               rm_sc[:, 2 * tc + 1:2 * tc + 2], boutu[:, :],
                                               ALU.mult, ALU.mult)
                        v.drain()
                        v.wait_ge(sm["silB"].h, d["sil"][tc])
                        n0 = d["td"][tc] - 1
                        if n0 >= 2:
                            psem = "dsp0" if n0 % 2 == 0 else "dsp1"
                            v.wait_ge(sm[psem].h, 16 * ((n0 - 2) // 2 + 1))
                        v.tensor_tensor(tseg[n0 % 2][:, :], sil[:, :], ubuf[:, :], ALU.mult)
                        v.drain()
                        if ip == 0:
                            v.tensor_reduce(rmt_raw[:, tc:tc + 1], tseg[n0 % 2][:, :], AX.X, ALU.max,
                                            apply_absolute_value=True)
                        else:
                            v.tensor_reduce(rmc[:, 1:2], tseg[n0 % 2][:, :], AX.X, ALU.max,
                                            apply_absolute_value=True)
                            v.drain()
                            v.tensor_tensor(rmt_raw[:, tc:tc + 1], rmt_raw[:, tc:tc + 1], rmc[:, 1:2], ALU.max)
                        v.drain()
                        v.sem_inc(sm["tdB"].h, 1)
                v.reciprocal(rmt_sc[:, 0:NTC], rmt_raw[:, :])
                v.drain()
                v.tensor_scalar(rmt_sc[:, 0:NTC], rmt_sc[:, 0:NTC], 224.0, None, ALU.mult)
                v.tensor_scalar(rmt_sc[:, NTC:2 * NTC], rmt_raw[:, :], Q2, None, ALU.mult)
                v.drain()
                v.sem_inc(sm["vC"].h, 1)

            @blk.scalar
            def _(s):
                nsil = 0
                for d in SCH:
                    for tc in range(NTC):
                        s.wait_ge(sm["evB"].h, d["ev"][tc])
                        if nsil >= 1:
                            s.wait_ge(sm["tdB"].h, nsil)
                        s.activation(sil[:, :], gbuf[:, tc * 512:(tc + 1) * 512], AF.Silu,
                                     bias=0.0, scale=1.0).then_inc(sm["silB"].h, 1)
                        nsil += 1
                        n0 = d["td"][tc] - 1
                        s.wait_ge(sm["tdB"].h, d["td"][tc])
                        psem = "dsp0" if n0 % 2 == 0 else "dsp1"
                        s.dma_start(t_spill[tc * 128:(tc + 1) * 128, d["ip"] * 512:(d["ip"] + 1) * 512],
                                    tseg[n0 % 2][:, :]).then_inc(sm[psem].h, 16)

            @blk.tensor
            def _(t):
                for d in SCH:
                    ip = d["ip"]
                    for w in ("g", "u"):
                        for tc in range(NTC):
                            if w == "g" and ip > 0:
                                t.wait_ge(sm["tdB"].h, SCH[ip - 1]["td"][tc])
                            if w == "u":
                                t.wait_ge(sm["evB"].h, d["ev"][tc])
                            for fp in range(NFP):
                                t.wait_ge(sm["qwB"].h, d["qw" + w][fp])
                                m = t.matmul(PS[tc][:, :],
                                             dr_lhsT(qaT_t, NTC * DM, tc * DM + fp * 256),
                                             dr_rhs(fp),
                                             start=(fp == 0), stop=(fp == NFP - 1),
                                             perf_mode=DR)
                                if fp == NFP - 1:
                                    m.then_inc(sm["mmB"].h, 1)

        stA.close()

        stPB.close()
        stPC = contextlib.ExitStack()
        pst = stPC.enter_context(nc.psum_tensor("pstC", [128, 256], FP8))
        stC = contextlib.ExitStack()
        tw = stC.enter_context(nc.sbuf_tensor("tw", [128, DI], F32))
        qt_t = stC.enter_context(nc.sbuf_tensor("qt", [128, DI], FP8))
        qtT_t = stC.enter_context(nc.sbuf_tensor("qtT", [128, NTC * DI], FP8))

        with nc.Block() as blk:
            @blk.sync
            def _(sync):
                nspill = NIP * NTC
                sync.wait_ge(sm["dsp0"].h, 16 * ((nspill + 1) // 2))
                sync.wait_ge(sm["dsp1"].h, 16 * (nspill // 2))
                for tc in range(NTC):
                    if tc >= 1:
                        sync.wait_ge(sm["vC"].h, 1 + tc)
                    sync.dma_start(tw[:, :], t_spill[tc * 128:(tc + 1) * 128, :]).then_inc(sm["dtC"].h, 16)

            @blk.vector
            def _(v):
                for tc in range(NTC):
                    v.wait_ge(sm["dtC"].h, 16 * (tc + 1))
                    v.tensor_scalar(qt_t[:, :], tw[:, :], rmt_sc[:, tc:tc + 1], None, ALU.mult)
                    v.drain()
                    v.sem_inc(sm["vC"].h, 1)
                    for ic in range(DI // 128):
                        v.wait_ge(sm["tC"].h, tc * (DI // 128) + ic + 1)
                        ip2, i = divmod(ic, 2)
                        v.tensor_copy(qtT_t[:, tc * DI + ip2 * 256 + i * 128: tc * DI + ip2 * 256 + (i + 1) * 128],
                                      bass.AP(pst, 0, [[256, 128], [2, 128]])).then_inc(sm["evC"].h, 1)

            @blk.tensor
            def _(t):
                for tc in range(NTC):
                    t.wait_ge(sm["vC"].h, 2 + tc)
                    for ic in range(DI // 128):
                        n = tc * (DI // 128) + ic
                        if n >= 1:
                            t.wait_ge(sm["evC"].h, n)
                        t.transpose(bass.AP(pst, 0, [[256, 128], [2, 128]]),
                                    qt_t[:, ic * 128:(ic + 1) * 128], id8b[:, :]).then_inc(sm["tC"].h, 1)

        stPC.close()
        stPD = contextlib.ExitStack()
        ps8 = stPD.enter_context(nc.psum_tensor("ps8d", [128, 512], F32))
        PS = ps + [ps8]

        cqwD = Counter(); cmmD = Counter(); cevD = Counter(); crpD = Counter()
        SD = []
        for hp in range(NHP):
            d = {"hp": hp, "dw": [], "qw": [], "rp": crpD.reserve(16)}
            for ip2 in range(NP2):
                par = ip2 % 2
                d["dw"].append((f"dw{par}", cdw[par].reserve(32)))
                d["qw"].append(cqwD.reserve())
            d["mm"] = [cmmD.reserve() for _ in range(NTC)]
            d["ev"] = [cevD.reserve() for _ in range(NTC)]
            SD.append(d)

        with nc.Block() as blk:
            @blk.sync
            def _(sync):
                for d in SD:
                    hp = d["hp"]
                    if hp >= 1:
                        sync.wait_ge(sm["bcD"].h, hp)
                    sync.dma_start(rowp[:, 0:512], rwd_i[:, hp * 512:(hp + 1) * 512]).then_inc(sm["drp"].h, 16)

            @blk.gpsimd
            def _(g):
                base = crp.n
                for d in SD:
                    hp = d["hp"]
                    g.wait_ge(sm["drp"].h, base + d["rp"])
                    if hp > 0:
                        g.wait_ge(sm["qwD"].h, SD[hp - 1]["qw"][-1])
                        g.wait_ge(sm["evD"].h, SD[hp - 1]["ev"][-1])
                    g.partition_broadcast(boutg[:, :], rowp[0:1, 0:512], channels=128)
                    g.drain()
                    g.sem_inc(sm["bcD"].h, 1)
                    for ip2 in range(NP2):
                        buf = ip2 % 2
                        qm = d["qw"][ip2]
                        if qm - 2 >= 1:
                            g.wait_ge(sm["qwD"].h, qm - 2)
                        dsn, dsv = d["dw"][ip2]
                        g.dma_start(ws[buf][:, 0:512],
                                    wd2T[ip2 * 256:ip2 * 256 + 128, hp * 512:(hp + 1) * 512]).then_inc(sm[dsn].h, 16)
                        g.dma_start(ws[buf][:, 512:1024],
                                    wd2T[ip2 * 256 + 128:ip2 * 256 + 256, hp * 512:(hp + 1) * 512]).then_inc(sm[dsn].h, 16)

            @blk.vector
            def _(v):
                for d in SD:
                    hp = d["hp"]
                    v.wait_ge(sm["bcD"].h, hp + 1)
                    if hp > 0:
                        v.wait_ge(sm["mmD"].h, SD[hp - 1]["mm"][-1])
                    v.reciprocal(binv[:, :], boutg[:, :])
                    v.drain()
                    v.tensor_scalar(binv[:, :], binv[:, :], 224.0, None, ALU.mult)
                    v.drain()
                    for ip2 in range(NP2):
                        buf = ip2 % 2
                        dsn, dsv = d["dw"][ip2]
                        v.wait_ge(sm[dsn].h, dsv)
                        v.tensor_tensor(qwf[:, ip2 * 1024:ip2 * 1024 + 512], ws[buf][:, 0:512], binv[:, :], ALU.mult)
                        v.tensor_tensor(qwf[:, ip2 * 1024 + 512:(ip2 + 1) * 1024], ws[buf][:, 512:1024], binv[:, :], ALU.mult)
                        v.drain()
                        v.sem_inc(sm["qwD"].h, 1)
                    for tc in range(NTC):
                        v.wait_ge(sm["mmD"].h, d["mm"][tc])
                        v.scalar_tensor_tensor(y2[:, tc * DM + hp * 512: tc * DM + (hp + 1) * 512], PS[tc][:, :],
                                               rmt_sc[:, NTC + tc:NTC + tc + 1], boutg[:, :],
                                               ALU.mult, ALU.mult)
                        v.drain()
                        v.sem_inc(sm["evD"].h, 1)

            @blk.tensor
            def _(t):
                for d in SD:
                    hp = d["hp"]
                    for tc in range(NTC):
                        if hp > 0:
                            t.wait_ge(sm["evD"].h, SD[hp - 1]["ev"][tc])
                        for ip2 in range(NP2):
                            t.wait_ge(sm["qwD"].h, d["qw"][ip2])
                            m = t.matmul(PS[tc][:, :],
                                         dr_lhsT(qtT_t, NTC * DI, tc * DI + ip2 * 256),
                                         dr_rhs(ip2),
                                         start=(ip2 == 0), stop=(ip2 == NP2 - 1),
                                         perf_mode=DR)
                            if ip2 == NP2 - 1:
                                m.then_inc(sm["mmD"].h, 1)

        stC.close()

        stE = contextlib.ExitStack()
        yTb = [stE.enter_context(nc.sbuf_tensor(f"yTb{i}", [128, TPC], BF16)) for i in range(2)]
        wob = [stE.enter_context(nc.sbuf_tensor(f"wob{i}", [128, 512], BF16)) for i in range(2)]

        cmmE = Counter(); cevE = Counter(); cwu = Counter(); cyu = Counter()
        cdE = {0: Counter(), 1: Counter()}
        cdY = {0: Counter(), 1: Counter()}
        SE = []
        for op in range(NHP):
            d = {"op": op, "dw": [], "dy": [], "wu": [], "yu": []}
            for dc in range(NDC):
                par = dc % 2
                d["dw"].append((f"dwE{par}", cdE[par].reserve(16)))
                d["dy"].append((f"dyE{par}", cdY[par].reserve(16)))
                wm = cwu.reserve() if dc < NDC - 1 else None
                d["wu"].append(wm)
                d["yu"].append(wm)
            d["mm"] = [cmmE.reserve() for _ in range(NTC)]
            d["ev"] = [cevE.reserve() for _ in range(NTC)]
            SE.append(d)

        with nc.Block() as blk:
            @blk.gpsimd
            def _(g):
                for op in range(NHP):
                    d = SE[op]
                    for dc in range(NDC):
                        n = op * NDC + dc
                        if n >= 2:
                            pn = n - 2
                            pd = SE[pn // NDC]
                            if pd["wu"][pn % NDC] is not None:
                                g.wait_ge(sm["bufE"].h, pd["wu"][pn % NDC])
                            else:
                                g.wait_ge(sm["evE"].h, pd["ev"][-1])
                        dsn, dsv = d["dw"][dc]
                        g.dma_start(wob[dc % 2][:, :],
                                    w_out[dc * 128:(dc + 1) * 128, op * 512:(op + 1) * 512]).then_inc(sm[dsn].h, 16)

            @blk.sync
            def _(sync):
                for op in range(NHP):
                    d = SE[op]
                    for dc in range(NDC):
                        n = op * NDC + dc
                        if n >= 2:
                            pn = n - 2
                            pd = SE[pn // NDC]
                            if pd["yu"][pn % NDC] is not None:
                                sync.wait_ge(sm["bufE"].h, pd["yu"][pn % NDC])
                            else:
                                sync.wait_ge(sm["evE"].h, pd["ev"][-1])
                        dsn, dsv = d["dy"][dc]
                        sync.dma_start(yTb[dc % 2][:, :],
                                    yT[dc * 128:(dc + 1) * 128, :]).then_inc(sm[dsn].h, 16)

            @blk.tensor
            def _(t):
                for op in range(NHP):
                    d = SE[op]
                    for dc in range(NDC):
                        dsn, dsv = d["dw"][dc]
                        t.wait_ge(sm[dsn].h, dsv)
                        dyn, dyv = d["dy"][dc]
                        t.wait_ge(sm[dyn].h, dyv)
                        for tc in range(NTC):
                            if dc == 0 and op > 0:
                                t.wait_ge(sm["evE"].h, SE[op - 1]["ev"][tc])
                            m = t.matmul(PS[tc][:, :],
                                         yTb[dc % 2][:, tc * 128:(tc + 1) * 128],
                                         wob[dc % 2][:, :],
                                         start=(dc == 0), stop=(dc == NDC - 1))
                            if tc == NTC - 1 and dc < NDC - 1:
                                m.then_inc(sm["bufE"].h, 1)
                            if dc == NDC - 1:
                                m.then_inc(sm["mmE"].h, 1)

            @blk.vector
            def _(v):
                for op in range(NHP):
                    d = SE[op]
                    for tc in range(NTC):
                        v.wait_ge(sm["mmE"].h, d["mm"][tc])
                        v.tensor_tensor(y2[:, tc * DM + op * 512: tc * DM + (op + 1) * 512],
                                        y2[:, tc * DM + op * 512: tc * DM + (op + 1) * 512],
                                        PS[tc][:, :], ALU.add)
                        v.drain()
                        v.sem_inc(sm["evE"].h, 1)

        stE.close()
        stPD.close()

        with nc.Block() as blk:
            @blk.sync
            def _(sync):
                for tc in range(NTC):
                    if tc >= 1:
                        sync.wait_ge(sm["vF"].h, tc)
                    sync.dma_start(xw[:, :], x[tc * 128:(tc + 1) * 128, :]).then_inc(sm["dxF"].h, 16)

            @blk.vector
            def _(v):
                for tc in range(NTC):
                    v.wait_ge(sm["dxF"].h, 16 * (tc + 1))
                    if tc >= 2:
                        v.wait_ge(sm["doF0" if tc % 2 == 0 else "doF1"].h, 16 * ((tc - 2) // 2 + 1))
                    v.tensor_tensor(ostg[tc % 2][:, :], xw[:, :], y2[:, tc * DM:(tc + 1) * DM], ALU.add)
                    v.drain()
                    v.sem_inc(sm["vF"].h, 1)

            @blk.gpsimd
            def _(g):
                for tc in range(NTC):
                    g.wait_ge(sm["vF"].h, tc + 1)
                    dsem = sm["doF0"] if tc % 2 == 0 else sm["doF1"]
                    g.dma_start(out_o[tc * 128:(tc + 1) * 128, :], ostg[tc % 2][:, :]).then_inc(dsem.h, dsem.inc(16))
                g.wait_ge(sm["doF0"].h, sm["doF0"].n)
                g.wait_ge(sm["doF1"].h, sm["doF1"].n)
    return nc


_NC_CACHE = {}

def _get_nc(name, builder):
    if name not in _NC_CACHE:
        nc = builder()
        nc.compile()
        _NC_CACHE[name] = nc
    return _NC_CACHE[name]


def kernel(L0_attn_x, L0_attn_w_qkv, L0_attn_w_out, L0_attn_q_gain,
           L0_ffn_w_gate, L0_ffn_w_up, L0_ffn_w_down,
           L0_ffn_up_reorder_index, L0_ffn_down_reorder_index):
    import ml_dtypes
    from concourse.bass_utils import run_bass_kernel_spmd

    x_full = np.ascontiguousarray(np.asarray(L0_attn_x)[0])
    w_qkv = np.ascontiguousarray(np.asarray(L0_attn_w_qkv))
    w_out = np.ascontiguousarray(np.asarray(L0_attn_w_out))
    q_gain = np.asarray(L0_attn_q_gain)
    wg = np.ascontiguousarray(np.asarray(L0_ffn_w_gate))
    wu = np.ascontiguousarray(np.asarray(L0_ffn_w_up))
    wd = np.ascontiguousarray(np.asarray(L0_ffn_w_down))
    di_idx = np.asarray(L0_ffn_down_reorder_index)
    wd2 = np.ascontiguousarray(wd[:, np.argsort(di_idx)])

    half = 64
    inv_freq = (10000.0 ** (-(np.arange(half, dtype=np.float32)) / half)).astype(np.float32)
    theta = (np.arange(S, dtype=np.float32)[:, None] * inv_freq[None, :]).astype(np.float32)
    cos_full = np.cos(theta).astype(np.float32)
    sin_full = np.sin(theta).astype(np.float32)
    ident = np.eye(128, dtype=np.float32).astype(ml_dtypes.bfloat16)
    id8 = np.eye(128, dtype=np.float32).astype(ml_dtypes.float8_e4m3)
    gain_b = np.ascontiguousarray(np.broadcast_to(q_gain[None, :].astype(np.float32), (128, NH)))

    nc1 = _get_nc("l1", build_l1)
    in1 = []
    for c in range(NCORE):
        in1.append({
            "x": np.ascontiguousarray(x_full[c * TPC:(c + 1) * TPC]),
            "w_qkv": w_qkv,
            "cos_t": np.ascontiguousarray(cos_full[c * TPC:(c + 1) * TPC]),
            "sin_t": np.ascontiguousarray(sin_full[c * TPC:(c + 1) * TPC]),
            "gain_b": gain_b,
            "ident": ident,
            "wg_sh": np.ascontiguousarray(wg[c * (DI // 8):(c + 1) * (DI // 8)]),
            "wu_sh": np.ascontiguousarray(wu[c * (DI // 8):(c + 1) * (DI // 8)]),
            "wd_sh": np.ascontiguousarray(wd2[c * (DM // 8):(c + 1) * (DM // 8)]),
        })
    r1 = run_bass_kernel_spmd(nc1, in1, core_ids=list(range(NCORE)))
    qkvn_full = np.concatenate([np.asarray(r1.results[c]["qkvn"]) for c in range(NCORE)], axis=0)
    xn_full = np.concatenate([np.asarray(r1.results[c]["xn_o"]) for c in range(NCORE)], axis=0)
    rwg = np.concatenate([np.asarray(r1.results[c]["rwg_o"]) for c in range(NCORE)])
    rwu = np.concatenate([np.asarray(r1.results[c]["rwu_o"]) for c in range(NCORE)])
    rwd = np.concatenate([np.asarray(r1.results[c]["rwd_o"]) for c in range(NCORE)])

    jj = np.arange(512)[None, :]
    tt = np.arange(128)[:, None]
    masks = [(jj >= j * 128 + tt).astype(np.float32) for j in range(4)]
    m01 = np.ascontiguousarray(np.concatenate(masks[0:2], 1)).astype(ml_dtypes.bfloat16)
    m23 = np.ascontiguousarray(np.concatenate(masks[2:4], 1)).astype(ml_dtypes.bfloat16)

    nc2 = _get_nc("l2", build_l2)
    in2 = []
    for c in range(NCORE):
        qs = []
        ks = []
        for hh in range(HPC):
            h = HPC * c + hh
            qs.append(np.ascontiguousarray(qkvn_full[:, h * 128:(h + 1) * 128].T))
            ks.append(np.ascontiguousarray(qkvn_full[:, DM + h * 128:DM + (h + 1) * 128].T))
        v_tm = np.ascontiguousarray(qkvn_full[:, 2 * DM + HPC * c * 128: 2 * DM + HPC * (c + 1) * 128])
        in2.append({
            "q_in": np.stack(qs), "k_in": np.stack(ks), "v_in": v_tm,
            "m01": m01, "m23": m23,
        })
    r2 = run_bass_kernel_spmd(nc2, in2, core_ids=list(range(NCORE)))
    y_all = np.concatenate([np.asarray(r2.results[c]["y_o"]) for c in range(NCORE)], axis=0)
    yT_full = y_all.reshape(DM, S)

    nc3 = _get_nc("l3", build_l3)
    wgT_np = np.ascontiguousarray(wg.T)
    wuT_np = np.ascontiguousarray(wu.T)
    wd2T_np = np.ascontiguousarray(wd2.T)
    in3 = []
    for c in range(NCORE):
        in3.append({
            "x": np.ascontiguousarray(x_full[c * TPC:(c + 1) * TPC]),
            "xn": np.ascontiguousarray(xn_full[c * TPC:(c + 1) * TPC]),
            "yT": np.ascontiguousarray(yT_full[:, c * TPC:(c + 1) * TPC]),
            "w_out": w_out,
            "wgT": wgT_np, "wuT": wuT_np, "wd2T": wd2T_np,
            "rwg_i": np.ascontiguousarray(rwg[None, :]),
            "rwu_i": np.ascontiguousarray(rwu[None, :]),
            "rwd_i": np.ascontiguousarray(rwd[None, :]),
            "id8": id8,
        })
    r3 = run_bass_kernel_spmd(nc3, in3, core_ids=list(range(NCORE)))
    out = np.concatenate([np.asarray(r3.results[c]["out_o"]) for c in range(NCORE)], axis=0)
    return out[None, :, :].astype(np.float32)
